# version 01
# baseline (speedup 1.0000x reference)
import contextlib

import numpy as np
import ml_dtypes

import concourse.bass as bass
import concourse.mybir as mybir
from concourse.bass import IndirectOffsetOnAxis
from concourse.bass_utils import run_bass_kernel_spmd

F32 = mybir.dt.float32
BF16 = mybir.dt.bfloat16
I32 = mybir.dt.int32
U32 = mybir.dt.uint32
AF = mybir.ActivationFunctionType
OP = mybir.AluOpType

NCORES = 8
S = 8192
D = 4096
DH = 128
HM = 4
TMY = 1024
NKC = 32
NTS = 16
NOC = 32
EPS = 1e-5
ISQ = float(1.0 / np.sqrt(DH))
ENGINES = ("sync", "gpsimd", "vector", "scalar", "tensor")


def build_nc(debug=False):
    nc = bass.Bass("TRN2", target_bir_lowering=False)

    x_my = nc.dram_tensor("x_my", [TMY, D], F32, kind="ExternalInput").ap()
    qc = nc.dram_tensor("qc", [HM, 128, NKC, 128], I32, kind="ExternalInput").ap()
    qs_ = nc.dram_tensor("qs_", [HM, NKC, 128], F32, kind="ExternalInput").ap()
    kc_ = nc.dram_tensor("kc_", [1, 128, NKC, 128], I32, kind="ExternalInput").ap()
    ks_ = nc.dram_tensor("ks_", [1, NKC, 128], F32, kind="ExternalInput").ap()
    vc_ = nc.dram_tensor("vc_", [1, 128, NKC, 128], I32, kind="ExternalInput").ap()
    vs_ = nc.dram_tensor("vs_", [1, NKC, 128], F32, kind="ExternalInput").ap()
    oc_ = nc.dram_tensor("oc_", [NOC, 128, NKC, 128], I32, kind="ExternalInput").ap()
    os_ = nc.dram_tensor("os_", [NOC, NKC, 128], F32, kind="ExternalInput").ap()
    wker = nc.dram_tensor("wker", [NOC, 128, NKC, 128], F32, kind="ExternalInput").ap()
    bias_t = nc.dram_tensor("bias_t", [128, 32], F32, kind="ExternalInput").ap()
    gate_in = nc.dram_tensor("gate_in", [1, 16], F32, kind="ExternalInput").ap()
    la_in = nc.dram_tensor("la_in", [16, 32, D], F32, kind="ExternalInput").ap()
    lb_in = nc.dram_tensor("lb_in", [16, 128, 32, 32], F32, kind="ExternalInput").ap()
    las_in = nc.dram_tensor("las_in", [32, D], F32, kind="ExternalInput").ap()
    lbs_in = nc.dram_tensor("lbs_in", [128, 32, 32], F32, kind="ExternalInput").ap()
    rope_c = nc.dram_tensor("rope_c", [128, S], F32, kind="ExternalInput").ap()
    rope_s = nc.dram_tensor("rope_s", [128, S], F32, kind="ExternalInput").ap()
    masks_in = nc.dram_tensor("masks_in", [128, 2048], BF16, kind="ExternalInput").ap()
    ident_f_in = nc.dram_tensor("ident_f_in", [128, 128], F32, kind="ExternalInput").ap()
    ident_b_in = nc.dram_tensor("ident_b_in", [128, 128], BF16, kind="ExternalInput").ap()
    gidx = nc.dram_tensor("gidx", [128, 32], U32, kind="ExternalInput").ap()

    out_my = nc.dram_tensor("out_my", [TMY, D], F32, kind="ExternalOutput").ap()
    dumps = {}
    if debug:
        for nm, shp, dt in [
            ("d_xn", [TMY, D], BF16), ("d_q", [512, S], BF16),
            ("d_g", [D, TMY], F32), ("d_attn", [512, S], BF16),
            ("d_sg", [D, TMY], BF16),
        ]:
            dumps[nm] = nc.dram_tensor(nm, shp, dt, kind="ExternalOutput").ap()

    xn_t_scr = nc.dram_tensor("xn_t_scr", [TMY, D], BF16).ap()
    xnf_bounce = nc.dram_tensor("xnf_bounce", [D, TMY], BF16).ap()
    xnf_all = nc.dram_tensor("xnf_all", [NCORES * D, TMY], BF16).ap()
    q_scr = nc.dram_tensor("q_scr", [512, S], BF16).ap()
    g_scr = nc.dram_tensor("g_scr", [D, TMY], F32).ap()
    attn_bounce = nc.dram_tensor("attn_bounce", [512, S], BF16).ap()
    attn_all = nc.dram_tensor("attn_all", [NCORES * 512, S], BF16).ap()
    sg_scr = nc.dram_tensor("sg_scr", [D, TMY], BF16).ap()

    ctx = contextlib.ExitStack()
    with ctx:
        sb = lambda name, shape, dt: ctx.enter_context(nc.sbuf_tensor(name, shape, dt))
        AB = sb("arena_bf", [128, 59392], BF16)
        AFp = sb("arena_f32", [128, 12288], F32)

        XNF = AB[:, 0:32768]
        FFNW = [AB[:, 32768 + i * 4096:32768 + (i + 1) * 4096] for i in (0, 1)]
        AEFT = AB[:, 40960:43008]
        BEFT = AB[:, 43008:47104]
        BEFT2 = AB[:, 49152:53248]
        AXB = AB[:, 47104:48128]
        AXB2 = AB[:, 48128:49152]
        WDEQ = AB[:, 0:24576]
        SLAB = AB[:, 24576:40960]
        KSB = AB[:, 40960:49152]
        VT = AB[:, 49152:57344]
        VSB = AB[:, 57344:57856]
        QEV = [AB[:, 57856 + i * 512:57856 + (i + 1) * 512] for i in (0, 1)]
        ATTR = AB[:, 0:32768]
        QLD = [AB[:, 32768 + i * 2048:32768 + (i + 1) * 2048] for i in (0, 1)]
        PRB = [AB[:, 36864 + i * 512:36864 + (i + 1) * 512] for i in (0, 1)]
        ATTM = AB[:, 24576:57344]
        OCD = [AB[:, i * 4096:(i + 1) * 4096] for i in (0, 1)]
        WOD = [AB[:, 8192 + i * 4096:8192 + (i + 1) * 4096] for i in (0, 1)]
        SGB = [AB[:, 16384 + i * 512:16384 + (i + 1) * 512] for i in (0, 1)]
        SGT = [AB[:, i * 4096:(i + 1) * 4096] for i in (0, 1)]
        XT = [AFp[:, i * 4096:(i + 1) * 4096] for i in (0, 1)]
        SQS = AFp[:, 8192:12288]
        LAC = [AFp[:, i * 4096:(i + 1) * 4096] for i in (0, 1)]
        LLD = AFp[:, 8192:12288]
        BAC = [AFp[:, i * 1024:(i + 1) * 1024] for i in (0, 1)]
        BLD = AFp[:, 8192:9216]
        GEV = [AFp[:, 4096 + i * 512:4096 + (i + 1) * 512] for i in (0, 1)]
        RCS = [AFp[:, i * 512:(i + 1) * 512] for i in range(4)]
        RT1 = AFp[:, 2048:2560]
        RT2 = AFp[:, 2560:3072]
        RBS = AFp[:, 3072:3584]
        GLD = [AFp[:, i * 512:(i + 1) * 512] for i in (0, 1)]
        XTF = AFp[:, 0:4096]
        OUTF = AFp[:, 4096:8192]

        ident_f = sb("ident_f", [128, 128], F32)
        ident_b = sb("ident_b", [128, 128], BF16)
        masks = sb("masks", [128, 2048], BF16)
        biasT = sb("biasT", [128, 32], F32)
        gate_sb = sb("gate_sb", [1, 16], F32)
        gate_cols = sb("gate_cols", [128, 16], F32)
        scl_bf = sb("scl_bf", [1, 4096], BF16)
        halfones = sb("halfones", [1, 128], BF16)
        ones_b = sb("ones_b", [1, 128], BF16)
        ones_f = sb("ones_f", [1, 128], F32)
        ones_col = sb("ones_col", [128, 1], BF16)
        rec_f = sb("rec_f", [1, 512], F32)
        rec_b = sb("rec_b", [1, 512], BF16)
        idx_sb = sb("idx_sb", [128, 32], U32)
        stat = {nm: sb("st_" + nm, [128, 1], F32)
                for nm in ("sum", "ssq", "mean", "ex2", "varep", "sd", "rstd", "nbias")}

        psf = [ctx.enter_context(nc.psum_tensor(f"psf{i}", [128, 512], F32))
               for i in range(7)]
        psb = ctx.enter_context(nc.psum_tensor("psb", [128, 1024], BF16))

        sems = {nm: ctx.enter_context(nc.semaphore("sem_" + nm))
                for nm in ("sync", "gp", "cc", "t", "s", "v")}

        ops = {e: [] for e in ENGINES}
        cnt = {}

        def _inc(name, n=1):
            cnt[name] = cnt.get(name, 0) + n
            return cnt[name]

        def _val(name):
            return cnt.get(name, 0)

        def em(eng, fn):
            ops[eng].append(fn)

        def wait(eng, sem_name, val):
            if val and val > 0:
                em(eng, lambda e, sm=sems[sem_name], v=val: e.wait_ge(sm, v))

        def dma(eng, out, in_, **kw):
            sem_name = "sync" if eng == "sync" else "gp"
            v = _inc(sem_name, 16)
            em(eng, lambda e, o=out, i=in_, sm=sems[sem_name], kw2=dict(kw):
               e.dma_start(out=o, in_=i, **kw2).then_inc(sm, 16))
            return v

        def op1(eng, sem_name, build):
            v = _inc(sem_name)
            em(eng, lambda e, sm=sems[sem_name], b=build: b(e).then_inc(sm, 1))
            return v

        def op0(eng, build):
            em(eng, lambda e, b=build: b(e))

        def barrier():
            vals = {k: _val(k) for k in ("sync", "gp", "cc", "t", "s", "v")}
            for eng in ENGINES:
                for k, v in vals.items():
                    wait(eng, k, v)

        lm_identf = dma("sync", ident_f[:], ident_f_in)
        lm_identb = dma("sync", ident_b[:], ident_b_in)
        dma("sync", masks[:], masks_in)
        lm_bias = dma("sync", biasT[:], bias_t)
        dma("sync", idx_sb[:], gidx)
        lm_gate = dma("gpsimd", gate_sb[:], gate_in)
        op0("vector", lambda e: e.memset(halfones[:], 0.5))
        op0("vector", lambda e: e.memset(ones_b[:], 1.0))
        op0("vector", lambda e: e.memset(ones_f[:], 1.0))
        consts_v = op1("vector", "v", lambda e: e.memset(ones_col[:], 1.0))

        xst = []
        for i in range(8):
            pp = i % 2
            lv = dma("sync", XT[pp][:], x_my[i * 128:(i + 1) * 128, :])
            wait("vector", "sync", lv)
            vsum = op1("vector", "v", lambda e, p=pp: e.reduce_sum(
                stat["sum"][:], XT[p][:], axis=mybir.AxisListType.X))
            zq = op1("vector", "v", lambda e: e.memset(stat["ssq"][:], 0.0))
            wait("scalar", "sync", lv)
            wait("scalar", "v", zq)
            ssq = op1("scalar", "s", lambda e, p=pp: e.activation(
                SQS[:], XT[p][:], AF.Square, accum_out=stat["ssq"][:]))
            wait("vector", "s", ssq)
            wait("vector", "v", vsum)
            vme = op1("vector", "v", lambda e: e.tensor_scalar_mul(
                stat["mean"][:], stat["sum"][:], 1.0 / D))
            vex = op1("vector", "v", lambda e: e.tensor_scalar_mul(
                stat["ex2"][:], stat["ssq"][:], 1.0 / D))
            wait("vector", "v", vme)
            vv1 = op1("vector", "v", lambda e: e.scalar_tensor_tensor(
                stat["varep"][:], stat["mean"][:], -1.0, stat["mean"][:], OP.mult, OP.mult))
            wait("vector", "v", max(vv1, vex))
            vvar = op1("vector", "v", lambda e: e.scalar_tensor_tensor(
                stat["varep"][:], stat["ex2"][:], EPS, stat["varep"][:], OP.add, OP.add))
            wait("scalar", "v", vvar)
            vsd = op1("scalar", "s", lambda e: e.activation(
                stat["sd"][:], stat["varep"][:], AF.Sqrt))
            wait("vector", "s", vsd)
            vrs = op1("vector", "v", lambda e: e.reciprocal(stat["rstd"][:], stat["sd"][:]))
            wait("vector", "v", vrs)
            vnb = op1("vector", "v", lambda e: e.scalar_tensor_tensor(
                stat["nbias"][:], stat["mean"][:], -1.0, stat["rstd"][:], OP.mult, OP.mult))
            wait("scalar", "v", vnb)
            if i > 0:
                wait("scalar", "sync", xst[-1])
            vxn = op1("scalar", "s", lambda e, p=pp: e.activation(
                FFNW[0][:, 0:4096], XT[p][:], AF.Identity,
                bias=stat["nbias"][:], scale=stat["rstd"][:]))
            wait("sync", "s", vxn)
            xst.append(dma("sync", xn_t_scr[i * 128:(i + 1) * 128, :], FFNW[0][:, 0:4096]))
        wait("sync", "sync", xst[-1])
        xnf_ready = 0
        for f in range(NKC):
            v = _inc("sync", 16)
            em("sync", lambda e, f2=f: e.dma_start_transpose(
                XNF[:, f2 * 1024:(f2 + 1) * 1024],
                xn_t_scr[0:1024, f2 * 128:(f2 + 1) * 128]).then_inc(sems["sync"], 16))
            xnf_ready = v
        wait("gpsimd", "sync", xnf_ready)
        bl = dma("gpsimd", xnf_bounce.rearrange("(f p) t -> p f t", p=128),
                 XNF.rearrange("p (f t) -> p f t", f=NKC))
        wait("gpsimd", "gp", bl)
        cc_xn = op1("gpsimd", "cc", lambda e: e.collective_compute(
            "AllGather", OP.bypass, replica_groups=[list(range(NCORES))],
            ins=[xnf_bounce.bitcast(F32).opt()],
            outs=[xnf_all.bitcast(F32).opt()]))

        wait("tensor", "gp", lm_gate)
        wait("tensor", "v", consts_v)
        tg = op1("tensor", "t", lambda e: e.matmul(
            psf[2][:, 0:16], ones_f[:], gate_sb[:], start=True, stop=True))
        wait("vector", "t", tg)
        vg = op1("vector", "v", lambda e: e.tensor_copy(gate_cols[:], psf[2][:, 0:16]))
        gate_bcast_done = vg

        prev_stt = 0
        for n in range(16):
            if prev_stt:
                wait("gpsimd", "v", prev_stt)
            lv = dma("gpsimd", LLD[0:32, :], la_in[n])
            wait("vector", "gp", lv)
            if n == 0:
                wait("vector", "v", vg)
                prev_stt = op1("vector", "v", lambda e: e.tensor_scalar_mul(
                    LAC[0][0:32, :], LLD[0:32, :], gate_cols[0:32, 0:1]))
            else:
                src, dst = LAC[(n + 1) % 2], LAC[n % 2]
                wait("vector", "v", prev_stt)
                prev_stt = op1("vector", "v", lambda e, s2=src, d2=dst, n2=n:
                               e.scalar_tensor_tensor(
                                   d2[0:32, :], LLD[0:32, :],
                                   gate_cols[0:32, n2:n2 + 1], s2[0:32, :],
                                   OP.mult, OP.add))
        a_own = LAC[1]
        aeft_done = 0
        wait("tensor", "sync", lm_identf)
        for kc in range(NKC):
            wait("tensor", "v", prev_stt if kc == 0 else aeft_done)
            tv = op1("tensor", "t", lambda e, k2=kc: e.transpose(
                psf[5][:, 0:32], a_own[0:32, k2 * 128:(k2 + 1) * 128],
                ident_f[0:32, 0:32]))
            wait("vector", "t", tv)
            aeft_done = op1("vector", "v", lambda e, k2=kc: e.tensor_copy(
                AEFT[:, k2 * 64:k2 * 64 + 32], psf[5][:, 0:32]))
        wait("gpsimd", "v", prev_stt)
        lv = dma("gpsimd", LLD[0:32, :], las_in)
        for kc in range(NKC):
            wait("tensor", "gp", lv)
            wait("tensor", "v", aeft_done)
            tv = op1("tensor", "t", lambda e, k2=kc: e.transpose(
                psf[5][:, 0:32], LLD[0:32, k2 * 128:(k2 + 1) * 128],
                ident_f[0:32, 0:32]))
            wait("vector", "t", tv)
            aeft_done = op1("vector", "v", lambda e, k2=kc: e.tensor_copy(
                AEFT[:, k2 * 64 + 32:k2 * 64 + 64], psf[5][:, 0:32]))

        prev_b = 0
        for n in range(16):
            if n == 0:
                wait("gpsimd", "v", aeft_done)
            else:
                wait("gpsimd", "v", prev_b)
            lv = dma("gpsimd", BLD[:, :], lb_in[n].rearrange("p a b -> p (a b)"))
            wait("vector", "gp", lv)
            if n == 0:
                prev_b = op1("vector", "v", lambda e: e.tensor_scalar_mul(
                    BAC[0][:], BLD[:], gate_cols[:, 0:1]))
            else:
                src, dst = BAC[(n + 1) % 2], BAC[n % 2]
                wait("vector", "v", prev_b)
                prev_b = op1("vector", "v", lambda e, s2=src, d2=dst, n2=n:
                             e.scalar_tensor_tensor(
                                 d2[:], BLD[:], gate_cols[:, n2:n2 + 1], s2[:],
                                 OP.mult, OP.add))
        b_own = BAC[1]
        wait("gpsimd", "v", prev_b)
        bls = dma("gpsimd", BLD[:, :], lbs_in.rearrange("p a b -> p (a b)"))
        beft_done = 0
        for oc in range(NOC):
            wait("tensor", "v", prev_b if oc == 0 else beft_done)
            tv1 = op1("tensor", "t", lambda e, o2=oc: e.transpose(
                psf[5][0:32, 0:128], b_own[:, o2 * 32:(o2 + 1) * 32], ident_f[:]))
            wait("vector", "t", tv1)
            ev1 = op1("vector", "v", lambda e, o2=oc: e.tensor_copy(
                BEFT[0:32, o2 * 128:(o2 + 1) * 128], psf[5][0:32, 0:128]))
            if oc == 0:
                wait("tensor", "gp", bls)
            wait("tensor", "v", ev1)
            tv2 = op1("tensor", "t", lambda e, o2=oc: e.transpose(
                psf[5][0:32, 0:128], BLD[:, o2 * 32:(o2 + 1) * 32], ident_f[:]))
            wait("vector", "t", tv2)
            beft_done = op1("vector", "v", lambda e, o2=oc: e.tensor_copy(
                BEFT2[0:32, o2 * 128:(o2 + 1) * 128], psf[5][0:32, 0:128]))

        ax_done = 0
        wait("tensor", "sync", xnf_ready)
        for st in range(2):
            for part, dstbuf, coff in ((0, AXB, 0), (1, AXB2, 32)):
                wait("tensor", "v", aeft_done if (st == 0 and part == 0) else ax_done)
                for kc in range(NKC):
                    last = kc == NKC - 1
                    b = (lambda e, k2=kc, s2=st, c3=coff: e.matmul(
                        psf[3][0:32, :], AEFT[:, k2 * 64 + c3:k2 * 64 + c3 + 32],
                        XNF[:, k2 * 1024 + s2 * 512:k2 * 1024 + s2 * 512 + 512],
                        start=(k2 == 0), stop=(k2 == NKC - 1), skip_group_check=True))
                    if last:
                        tv = op1("tensor", "t", b)
                    else:
                        op0("tensor", b)
                wait("vector", "t", tv)
                ax_done = op1("vector", "v", lambda e, s2=st, d2=dstbuf: e.tensor_copy(
                    d2[0:32, s2 * 512:(s2 + 1) * 512], psf[3][0:32, :]))
        wait("tensor", "v", ax_done)

        wait("scalar", "sync", lm_bias)
        wait("gpsimd", "sync", xst[-1])
        g_marks = []
        gst_marks = []
        wslab_mark = [0, 0]
        for oc in range(NOC):
            pp = oc % 2
            if wslab_mark[pp]:
                wait("gpsimd", "t", wslab_mark[pp])
            lv = dma("gpsimd", FFNW[pp][:].rearrange("p (a b) -> p a b", a=NKC), wker[oc])
            wait("tensor", "gp", lv)
            wait("tensor", "v", max(ax_done, beft_done))
            for st in range(2):
                bank = (oc * 2 + st) % 2
                if len(g_marks) >= 2:
                    wait("tensor", "s", g_marks[-2])
                for kc in range(NKC):
                    op0("tensor", lambda e, k2=kc, p2=pp, s2=st, b2=bank: e.matmul(
                        psf[b2][:, :], FFNW[p2][:, k2 * 128:(k2 + 1) * 128],
                        XNF[:, k2 * 1024 + s2 * 512:k2 * 1024 + s2 * 512 + 512],
                        start=(k2 == 0), stop=False, skip_group_check=True))
                op0("tensor", lambda e, o2=oc, s2=st, b2=bank: e.matmul(
                    psf[b2][:, :], BEFT[0:32, o2 * 128:(o2 + 1) * 128],
                    AXB[0:32, s2 * 512:(s2 + 1) * 512],
                    start=False, stop=False, skip_group_check=True))
                tmm = op1("tensor", "t", lambda e, o2=oc, s2=st, b2=bank: e.matmul(
                    psf[b2][:, :], BEFT2[0:32, o2 * 128:(o2 + 1) * 128],
                    AXB2[0:32, s2 * 512:(s2 + 1) * 512],
                    start=False, stop=True, skip_group_check=True))
                wslab_mark[pp] = tmm
                gp2 = (oc * 2 + st) % 2
                if len(gst_marks) >= 2:
                    wait("scalar", "sync", gst_marks[-2])
                wait("scalar", "t", tmm)
                gm = op1("scalar", "s", lambda e, o2=oc, b2=bank, g3=gp2: e.activation(
                    GEV[g3][:], psf[b2][:, :], AF.Silu,
                    bias=biasT[:, o2:o2 + 1], scale=1.0))
                g_marks.append(gm)
                wait("sync", "s", gm)
                gst_marks.append(dma(
                    "sync", g_scr[oc * 128:(oc + 1) * 128, st * 512:(st + 1) * 512],
                    GEV[gp2][:]))
        barrier()

        last_bcast_t = [0]

        def emit_dequant(dst, codes_ap, scales_ap, ocd_buf, stage_war=0):
            src_buf = ocd_buf
            if stage_war:
                wait("gpsimd", "v", stage_war)
            lv = dma("gpsimd", src_buf.rearrange("p (a b) -> p a b", a=NKC), codes_ap)
            if last_bcast_t[0]:
                wait("gpsimd", "t", last_bcast_t[0])
            sv = dma("gpsimd", scl_bf[:].rearrange("o (a b) -> o a b", a=NKC), scales_ap)
            vmark = 0
            for piece in range(8):
                wait("tensor", "gp", sv)
                if vmark:
                    wait("tensor", "v", vmark)
                tv = op1("tensor", "t", lambda e, p2=piece: e.matmul(
                    psf[2][:, :], halfones[:], scl_bf[:, p2 * 512:(p2 + 1) * 512],
                    start=True, stop=True, skip_group_check=True))
                last_bcast_t[0] = tv
                wait("vector", "t", tv)
                wait("vector", "gp", lv)
                vmark = op1("vector", "v", lambda e, p2=piece, d2=dst, s3=src_buf:
                            e.scalar_tensor_tensor(
                                d2[:, p2 * 512:(p2 + 1) * 512],
                                s3[:, p2 * 512:(p2 + 1) * 512],
                                -8.0, psf[2][:, :], OP.add, OP.mult))
            return vmark

        stg = SLAB[:, 0:4096]
        dqw = 0
        for mc in range(HM):
            dqw = emit_dequant(WDEQ[:, mc * 4096:(mc + 1) * 4096], qc[mc],
                               qs_[mc].unsqueeze(0), stg, stage_war=dqw)
        dqw = emit_dequant(WDEQ[:, 16384:20480], kc_[0], ks_[0].unsqueeze(0),
                           stg, stage_war=dqw)
        deq_done = emit_dequant(WDEQ[:, 20480:24576], vc_[0], vs_[0].unsqueeze(0),
                                stg, stage_war=dqw)
        slab_consumed0 = deq_done

        wait("sync", "cc", cc_xn)
        xa4 = xnf_all.rearrange("(j kc p) t -> j p kc t", p=128, kc=NKC)
        slab_consumed = 0
        rope_use = [0, 0]
        qev_store = [0, 0]
        vt_last = 0
        k_last = 0
        q_stores_done = 0
        pj_bank_war = [0, 0]
        for ts in range(NTS):
            j, tl = ts // 2, (ts % 2) * 512
            wait("sync", "t", slab_consumed)
            if ts == 0:
                wait("sync", "v", slab_consumed0)
            slv = dma("sync", SLAB[:].rearrange("p (a b) -> p a b", a=NKC),
                      xa4[j][:, :, tl:tl + 512])
            rpp = ts % 2
            if rope_use[rpp]:
                wait("sync", "v", rope_use[rpp])
            cv = dma("sync", RCS[rpp * 2][:], rope_c[:, ts * 512:(ts + 1) * 512])
            sv2 = dma("sync", RCS[rpp * 2 + 1][:], rope_s[:, ts * 512:(ts + 1) * 512])
            for m in range(6):
                bank = m % 2
                wcol = m * 4096 if m < 4 else (16384 if m == 4 else 20480)
                wait("tensor", "sync", slv)
                if ts == 0 and m == 0:
                    wait("tensor", "v", deq_done)
                wait("tensor", "v", pj_bank_war[bank])
                for kc in range(NKC):
                    last = kc == NKC - 1
                    b = (lambda e, k2=kc, w2=wcol, b2=bank: e.matmul(
                        psf[b2][:, :], WDEQ[:, w2 + k2 * 128:w2 + (k2 + 1) * 128],
                        SLAB[:, k2 * 512:(k2 + 1) * 512],
                        start=(k2 == 0), stop=(k2 == NKC - 1), skip_group_check=True))
                    if last:
                        tmm = op1("tensor", "t", b)
                    else:
                        op0("tensor", b)
                if m == 5:
                    slab_consumed = tmm
                if m < 5:
                    wait("vector", "t", tmm)
                    wait("vector", "sync", sv2)
                    op1("vector", "v", lambda e, b2=bank, r2=rpp: e.tensor_tensor(
                        RT1[:], psf[b2][:, :], RCS[r2 * 2][:], OP.mult))
                    op1("vector", "v", lambda e, b2=bank, r2=rpp: e.tensor_tensor(
                        RT2[0:64, :], psf[b2][64:128, :], RCS[r2 * 2 + 1][0:64, :], OP.mult))
                    vrt = op1("vector", "v", lambda e, b2=bank, r2=rpp: e.tensor_tensor(
                        RT2[64:128, :], psf[b2][0:64, :], RCS[r2 * 2 + 1][64:128, :], OP.mult))
                    wait("vector", "v", vrt)
                    if m < 4:
                        qp = m % 2
                        if qev_store[qp]:
                            wait("vector", "sync", qev_store[qp])
                        vev = op1("vector", "v", lambda e, q2=qp: e.tensor_add(
                            QEV[q2][:], RT1[:], RT2[:]))
                        wait("sync", "v", vev)
                        qev_store[qp] = dma(
                            "sync", q_scr[m * 128:(m + 1) * 128, ts * 512:(ts + 1) * 512],
                            QEV[qp][:])
                        q_stores_done = qev_store[qp]
                    else:
                        vev = op1("vector", "v", lambda e, t2=ts: e.tensor_add(
                            KSB[:, t2 * 512:(t2 + 1) * 512], RT1[:], RT2[:]))
                        k_last = vev
                    pj_bank_war[bank] = vev
                    rope_use[rpp] = vev
                else:
                    wait("vector", "t", tmm)
                    vev = op1("vector", "v", lambda e, b2=bank: e.tensor_copy(
                        VSB[:], psf[b2][:, :]))
                    pj_bank_war[bank] = vev
                    wait("tensor", "v", vev)
                    vtr = 0
                    for tb in range(4):
                        if vtr:
                            wait("tensor", "v", vtr)
                        ttr = op1("tensor", "t", lambda e, t3=tb: e.transpose(
                            psb[:, 0:128], VSB[:, t3 * 128:(t3 + 1) * 128], ident_b[:]))
                        wait("vector", "t", ttr)
                        vtr = op1("vector", "v", lambda e, t2=ts, t3=tb: e.tensor_copy(
                            VT[:, (t2 * 4 + t3) * 128:(t2 * 4 + t3 + 1) * 128],
                            psb[:, 0:128]))
                    vt_last = vtr
        kv_ready = max(k_last, vt_last)
        barrier()

        scdone = {}
        expdone = {}
        maskdone = {}
        jg = 0
        last_fin_v = 0
        last_tbc_s = 0
        last_tbc_t = 0
        qld_war = [0, 0]
        for qs in range(NTS):
            qpp = qs % 2
            q4 = q_scr.rearrange("(h p) t -> p h t", p=128)
            if qld_war[qpp]:
                wait("sync", "t", qld_war[qpp])
            wait("sync", "sync", q_stores_done)
            qlv = dma("sync", QLD[qpp][:].rearrange("p (a b) -> p a b", a=HM),
                      q4[:, :, qs * 512:(qs + 1) * 512])
            for hp in range(2):
                heads = (hp * 2, hp * 2 + 1)
                tiles = [(ks, hh, tb) for ks in range(qs + 1)
                         for hh in range(2) for tb in range(4)]
                n = len(tiles)
                first_av = [True, True]
                sweep_js = []

                def emit_scores(jj, tile):
                    ks, hh, tb = tile
                    h = heads[hh]
                    if jj - 2 in expdone:
                        wait("tensor", "s", expdone[jj - 2])
                    if jj == sweep_js[0]:
                        wait("tensor", "v", max(kv_ready, last_fin_v))
                        wait("tensor", "sync", qlv)
                    kb = ks * 4 + tb
                    scdone[jj] = op1("tensor", "t", lambda e, b2=jj % 2, k2=kb, h2=h,
                                     q2=qpp: e.matmul(
                        psf[b2][:, :], KSB[:, k2 * 128:(k2 + 1) * 128],
                        QLD[q2][:, h2 * 512:(h2 + 1) * 512],
                        start=True, stop=True, skip_group_check=True))

                def emit_av(jj, tile):
                    ks, hh, tb = tile
                    if ks == qs:
                        wait("tensor", "gp", maskdone[jj])
                    else:
                        wait("tensor", "s", expdone[jj])
                    kb = ks * 4 + tb
                    sf = first_av[hh]
                    first_av[hh] = False
                    op0("tensor", lambda e, k2=kb, h3=hh, p2=jj % 2, s2=sf: e.matmul(
                        psf[2 + h3][:, :], VT[:, k2 * 128:(k2 + 1) * 128],
                        PRB[p2][:], start=s2, stop=False, skip_group_check=True))
                    return op1("tensor", "t", lambda e, h3=hh, p2=jj % 2, s2=sf: e.matmul(
                        psf[4 + h3][0:1, :], ones_col[:], PRB[p2][:],
                        start=s2, stop=False, skip_group_check=True))

                for li, tile in enumerate(tiles):
                    jj = jg + li
                    sweep_js.append(jj)
                    emit_scores(jj, tile)
                    ks, hh, tb = tile
                    wait("scalar", "t", scdone[jj])
                    expdone[jj] = op1("scalar", "s", lambda e, p2=jj % 2, b2=jj % 2:
                                      e.activation(PRB[p2][:], psf[b2][:, :],
                                                   AF.Exp, scale=ISQ))
                    if ks == qs:
                        wait("gpsimd", "s", expdone[jj])
                        maskdone[jj] = op1("gpsimd", "gp", lambda e, p2=jj % 2, t3=tb:
                                           e.tensor_tensor(
                                               PRB[p2][:], PRB[p2][:],
                                               masks[:, t3 * 512:(t3 + 1) * 512],
                                               OP.mult))
                    if li >= 1:
                        emit_av(jg + li - 1, tiles[li - 1])
                av_last = emit_av(jg + n - 1, tiles[n - 1])
                qld_war[qpp] = av_last
                jg += n
                for hh in range(2):
                    h = heads[hh]
                    wait("vector", "t", av_last)
                    if last_tbc_t:
                        wait("vector", "t", last_tbc_t)
                    vrc = op1("vector", "v", lambda e, h3=hh: e.reciprocal(
                        rec_f[:], psf[4 + h3][0:1, :]))
                    wait("vector", "v", vrc)
                    vrec = op1("vector", "v", lambda e: e.tensor_copy(rec_b[:], rec_f[:]))
                    wait("tensor", "v", vrec)
                    if last_tbc_s:
                        wait("tensor", "s", last_tbc_s)
                    tbc = op1("tensor", "t", lambda e: e.matmul(
                        psf[6][:, :], ones_b[:], rec_b[:], start=True, stop=True,
                        skip_group_check=True))
                    last_tbc_t = tbc
                    wait("scalar", "t", tbc)
                    if last_fin_v:
                        wait("scalar", "v", last_fin_v)
                    last_tbc_s = op1("scalar", "s", lambda e: e.copy(RBS[:], psf[6][:, :]))
                    wait("vector", "s", last_tbc_s)
                    last_fin_v = op1("vector", "v", lambda e, h3=hh, h2=h, q2=qs:
                                     e.tensor_tensor(
                                         ATTR[:, (h2 * NTS + q2) * 512:
                                              (h2 * NTS + q2 + 1) * 512],
                                         psf[2 + h3][:, :], RBS[:], OP.mult))
        wait("gpsimd", "v", last_fin_v)
        abl = dma("gpsimd", attn_bounce.rearrange("(h p) t -> p h t", p=128),
                  ATTR.rearrange("p (h t) -> p h t", h=HM))
        wait("gpsimd", "gp", abl)
        cc_at = op1("gpsimd", "cc", lambda e: e.collective_compute(
            "AllGather", OP.bypass, replica_groups=[list(range(NCORES))],
            ins=[attn_bounce.bitcast(F32).opt()],
            outs=[attn_all.bitcast(F32).opt()]))
        barrier()

        wait("gpsimd", "cc", cc_at)
        att_virt = attn_all.rearrange("r (v c) -> (r v) c", c=1024)
        gather_done = 0
        for m in range(NKC):
            v = _inc("gp", 16)
            em("gpsimd", lambda e, m2=m: e.indirect_dma_start(
                ATTM[:, m2 * 1024:(m2 + 1) * 1024], None, att_virt,
                IndirectOffsetOnAxis(idx_sb[:, m2:m2 + 1], 0)).then_inc(sems["gp"], 16))
            gather_done = v

        sg_marks = []
        sgst_marks = []
        gld_war = [0, 0]
        ocd_war = [0, 0]
        wod_war = [0, 0]
        last_bcast_t[0] = 0
        for oc in range(NOC):
            pp = oc % 2
            if ocd_war[pp]:
                wait("gpsimd", "v", ocd_war[pp])
            if wod_war[pp]:
                wait("vector", "t", wod_war[pp])
            deq_mark = emit_dequant(WOD[pp], oc_[oc], os_[oc].unsqueeze(0),
                                    ocd_buf=OCD[pp])
            ocd_war[pp] = deq_mark
            for st in range(2):
                bank = st
                wait("tensor", "v", deq_mark)
                if oc == 0 and st == 0:
                    wait("tensor", "gp", gather_done)
                if len(sg_marks) >= 2:
                    wait("tensor", "v", sg_marks[-2])
                for kc in range(NKC):
                    last = kc == NKC - 1
                    b = (lambda e, k2=kc, c2=pp, s2=st, b2=bank: e.matmul(
                        psf[b2][:, :], WOD[c2][:, k2 * 128:(k2 + 1) * 128],
                        ATTM[:, k2 * 1024 + s2 * 512:k2 * 1024 + s2 * 512 + 512],
                        start=(k2 == 0), stop=(k2 == NKC - 1), skip_group_check=True))
                    if last:
                        tmm = op1("tensor", "t", b)
                    else:
                        op0("tensor", b)
                wod_war[pp] = tmm
                if gld_war[st]:
                    wait("sync", "v", gld_war[st])
                glv = dma("sync", GLD[st][:],
                          g_scr[oc * 128:(oc + 1) * 128, st * 512:(st + 1) * 512])
                sgp = (oc * 2 + st) % 2
                wait("vector", "t", tmm)
                wait("vector", "sync", glv)
                if len(sgst_marks) >= 2:
                    wait("vector", "sync", sgst_marks[-2])
                sgm = op1("vector", "v", lambda e, b2=bank, s2=st, c2=sgp: e.tensor_tensor(
                    SGB[c2][:], psf[b2][:, :], GLD[s2][:], OP.mult))
                sg_marks.append(sgm)
                gld_war[st] = sgm
                wait("sync", "v", sgm)
                sgst_marks.append(dma(
                    "sync", sg_scr[oc * 128:(oc + 1) * 128, st * 512:(st + 1) * 512],
                    SGB[sgp][:]))
        sg_stores_done = sgst_marks[-1]
        barrier()

        out_stored = 0
        sgt_war = [0, 0]
        prev_add = 0
        for tc in range(8):
            pp = tc % 2
            if tc == 0:
                wait("sync", "sync", sg_stores_done)
            if sgt_war[pp]:
                wait("sync", "v", sgt_war[pp])
            v = _inc("sync", 16)
            em("sync", lambda e, t2=tc, c2=pp: e.dma_start_transpose(
                SGT[c2][:], sg_scr[0:D, t2 * 128:(t2 + 1) * 128]).then_inc(sems["sync"], 16))
            sgtl = v
            if prev_add:
                wait("sync", "v", prev_add)
            xlv = dma("sync", XTF[:], x_my[tc * 128:(tc + 1) * 128, :])
            wait("vector", "sync", xlv)
            wait("vector", "sync", sgtl)
            if out_stored:
                wait("vector", "sync", out_stored)
            prev_add = op1("vector", "v", lambda e, c2=pp: e.tensor_add(
                OUTF[:], XTF[:], SGT[c2][:]))
            sgt_war[pp] = prev_add
            wait("sync", "v", prev_add)
            out_stored = dma("sync", out_my[tc * 128:(tc + 1) * 128, :], OUTF[:])

        if debug:
            wait("gpsimd", "sync", out_stored)
            dma("gpsimd", dumps["d_xn"], xn_t_scr)
            dma("gpsimd", dumps["d_q"], q_scr)
            dma("gpsimd", dumps["d_g"], g_scr)
            dma("gpsimd", dumps["d_attn"], attn_bounce)
            dv = dma("gpsimd", dumps["d_sg"], sg_scr)
            wait("gpsimd", "gp", dv)
        wait("sync", "sync", out_stored)

        block = ctx.enter_context(nc.Block())

        @block.sync
        def _(e):
            for f in ops["sync"]:
                f(e)

        @block.gpsimd
        def _(e):
            for f in ops["gpsimd"]:
                f(e)

        @block.vector
        def _(e):
            for f in ops["vector"]:
                f(e)

        @block.scalar
        def _(e):
            for f in ops["scalar"]:
                f(e)

        @block.tensor
        def _(e):
            for f in ops["tensor"]:
                f(e)

    return nc


def _host_prep(inputs):
    f32 = np.float32
    x = np.ascontiguousarray(np.asarray(inputs["L0_attn_x"], dtype=f32).reshape(S, D))
    qcodes = np.asarray(inputs["L0_attn_q_codes"], dtype=np.int32)
    kcodes = np.asarray(inputs["L0_attn_k_codes"], dtype=np.int32)
    vcodes = np.asarray(inputs["L0_attn_v_codes"], dtype=np.int32)
    ocodes = np.asarray(inputs["L0_attn_o_codes"], dtype=np.int32)
    qsc = np.asarray(inputs["L0_attn_q_scales"], dtype=f32)
    ksc = np.asarray(inputs["L0_attn_k_scales"], dtype=f32)
    vsc = np.asarray(inputs["L0_attn_v_scales"], dtype=f32)
    osc = np.asarray(inputs["L0_attn_o_scales"], dtype=f32)
    gate = np.asarray(inputs["L0_ffn_gate"], dtype=f32)
    kern = np.asarray(inputs["L0_ffn_kernel"], dtype=f32)
    bias = np.asarray(inputs["L0_ffn_bias"], dtype=f32)
    lA = np.ascontiguousarray(np.asarray(inputs["L0_ffn_lora_A"], dtype=f32))
    lB = np.asarray(inputs["L0_ffn_lora_B"], dtype=f32)
    lAs = np.ascontiguousarray(np.asarray(inputs["L0_ffn_lora_A_shared"], dtype=f32))
    lBs = np.asarray(inputs["L0_ffn_lora_B_shared"], dtype=f32)

    def colslab(codes, nslab):
        return np.ascontiguousarray(
            codes.reshape(NKC, 128, nslab, 128).transpose(2, 1, 0, 3))

    def sc_slab(scales, nslab):
        return np.ascontiguousarray(
            scales.reshape(NKC, nslab, 128).transpose(1, 0, 2))

    kern_h = colslab(kern, NOC)
    oc_h = colslab(ocodes, NOC)
    os_h = sc_slab(osc, NOC)
    bias_T = np.ascontiguousarray(bias.reshape(32, 128).T)
    lb_h = np.ascontiguousarray(lB.reshape(16, 32, 128, 32).transpose(0, 2, 1, 3))
    lbs_h = np.ascontiguousarray(lBs.reshape(32, 128, 32).transpose(1, 0, 2))

    half = DH // 2
    inv = 1.0 / (10000.0 ** (np.arange(half, dtype=np.float64) / half))
    ang = np.arange(S, dtype=np.float64)[:, None] * inv[None, :]
    cosT = np.cos(ang).T.astype(f32)
    sinT = np.sin(ang).T.astype(f32)
    rope_cos = np.ascontiguousarray(np.concatenate([cosT, cosT], axis=0))
    rope_sin = np.ascontiguousarray(np.concatenate([-sinT, sinT], axis=0))

    masks = np.zeros((128, 4, 512), dtype=f32)
    kk = np.arange(128)[:, None]
    qq = np.arange(512)[None, :]
    for tb in range(4):
        masks[:, tb, :] = (qq >= tb * 128 + kk).astype(f32)
    masks_b = np.ascontiguousarray(masks.reshape(128, 2048)).astype(ml_dtypes.bfloat16)

    ident = np.eye(128, dtype=f32)
    ident_b = ident.astype(ml_dtypes.bfloat16)

    qc_all = colslab(qcodes, 32)
    qs_all = sc_slab(qsc, 32)
    kc_all = colslab(kcodes, 8)
    ks_all = sc_slab(ksc, 8)
    vc_all = colslab(vcodes, 8)
    vs_all = sc_slab(vsc, 8)

    in_maps = []
    p = np.arange(128, dtype=np.uint32)[:, None]
    m = np.arange(32, dtype=np.uint32)[None, :]
    for c in range(NCORES):
        gidx_c = np.ascontiguousarray(((m * 128 + p) * 8 + c).astype(np.uint32))
        in_maps.append({
            "x_my": np.ascontiguousarray(x[c * TMY:(c + 1) * TMY]),
            "qc": np.ascontiguousarray(qc_all[c * HM:(c + 1) * HM]),
            "qs_": np.ascontiguousarray(qs_all[c * HM:(c + 1) * HM]),
            "kc_": np.ascontiguousarray(kc_all[c:c + 1]),
            "ks_": np.ascontiguousarray(ks_all[c:c + 1]),
            "vc_": np.ascontiguousarray(vc_all[c:c + 1]),
            "vs_": np.ascontiguousarray(vs_all[c:c + 1]),
            "oc_": oc_h,
            "os_": os_h,
            "wker": kern_h,
            "bias_t": bias_T,
            "gate_in": np.ascontiguousarray(gate[c:c + 1]),
            "la_in": lA,
            "lb_in": lb_h,
            "las_in": lAs,
            "lbs_in": lbs_h,
            "rope_c": rope_cos,
            "rope_s": rope_sin,
            "masks_in": masks_b,
            "ident_f_in": ident,
            "ident_b_in": ident_b,
            "gidx": gidx_c,
        })
    return in_maps


def _run(inputs, debug=False):
    in_maps = _host_prep(inputs)
    nc = build_nc(debug=debug)
    return run_bass_kernel_spmd(nc, in_maps, core_ids=list(range(NCORES)))


def kernel(**inputs):
    res = _run(inputs, debug=False)
    out = np.empty((1, S, D), dtype=np.float32)
    for c in range(NCORES):
        out[0, c * TMY:(c + 1) * TMY, :] = np.asarray(res.results[c]["out_my"])
    return out


# version 02 (final)
# speedup vs baseline: 1.0281x; 1.0281x over previous
import contextlib

import numpy as np
import ml_dtypes

import concourse.bass as bass
import concourse.mybir as mybir
from concourse.bass import IndirectOffsetOnAxis
from concourse.bass_utils import run_bass_kernel_spmd

F32 = mybir.dt.float32
BF16 = mybir.dt.bfloat16
I32 = mybir.dt.int32
U32 = mybir.dt.uint32
AF = mybir.ActivationFunctionType
OP = mybir.AluOpType

NCORES = 8
S = 8192
D = 4096
DH = 128
HM = 4
TMY = 1024
NKC = 32
NTS = 16
NOC = 32
EPS = 1e-5
ISQ = float(1.0 / np.sqrt(DH))
ENGINES = ("sync", "gpsimd", "vector", "scalar", "tensor")


def build_nc(debug=False):
    nc = bass.Bass("TRN2", target_bir_lowering=False)

    x_my = nc.dram_tensor("x_my", [TMY, D], F32, kind="ExternalInput").ap()
    qc = nc.dram_tensor("qc", [HM, 128, NKC, 128], I32, kind="ExternalInput").ap()
    qs_ = nc.dram_tensor("qs_", [HM, NKC, 128], F32, kind="ExternalInput").ap()
    kc_ = nc.dram_tensor("kc_", [1, 128, NKC, 128], I32, kind="ExternalInput").ap()
    ks_ = nc.dram_tensor("ks_", [1, NKC, 128], F32, kind="ExternalInput").ap()
    vc_ = nc.dram_tensor("vc_", [1, 128, NKC, 128], I32, kind="ExternalInput").ap()
    vs_ = nc.dram_tensor("vs_", [1, NKC, 128], F32, kind="ExternalInput").ap()
    oc_ = nc.dram_tensor("oc_", [NOC, 128, NKC, 128], I32, kind="ExternalInput").ap()
    os_ = nc.dram_tensor("os_", [NOC, NKC, 128], F32, kind="ExternalInput").ap()
    wker = nc.dram_tensor("wker", [NOC, 128, NKC, 128], F32, kind="ExternalInput").ap()
    bias_t = nc.dram_tensor("bias_t", [128, 32], F32, kind="ExternalInput").ap()
    gate_in = nc.dram_tensor("gate_in", [1, 16], F32, kind="ExternalInput").ap()
    la_in = nc.dram_tensor("la_in", [16, 32, D], F32, kind="ExternalInput").ap()
    lb_in = nc.dram_tensor("lb_in", [16, 128, 32, 32], F32, kind="ExternalInput").ap()
    las_in = nc.dram_tensor("las_in", [32, D], F32, kind="ExternalInput").ap()
    lbs_in = nc.dram_tensor("lbs_in", [128, 32, 32], F32, kind="ExternalInput").ap()
    rope_c = nc.dram_tensor("rope_c", [128, S], F32, kind="ExternalInput").ap()
    rope_s = nc.dram_tensor("rope_s", [128, S], F32, kind="ExternalInput").ap()
    masks_in = nc.dram_tensor("masks_in", [128, 2048], BF16, kind="ExternalInput").ap()
    ident_f_in = nc.dram_tensor("ident_f_in", [128, 128], F32, kind="ExternalInput").ap()
    ident_b_in = nc.dram_tensor("ident_b_in", [128, 128], BF16, kind="ExternalInput").ap()
    gidx = nc.dram_tensor("gidx", [128, 32], U32, kind="ExternalInput").ap()

    out_my = nc.dram_tensor("out_my", [TMY, D], F32, kind="ExternalOutput").ap()
    dumps = {}
    if debug:
        for nm, shp, dt in [
            ("d_xn", [TMY, D], BF16), ("d_q", [512, S], BF16),
            ("d_g", [D, TMY], F32), ("d_attn", [512, S], BF16),
            ("d_sg", [D, TMY], BF16),
        ]:
            dumps[nm] = nc.dram_tensor(nm, shp, dt, kind="ExternalOutput").ap()

    xn_t_scr = nc.dram_tensor("xn_t_scr", [TMY, D], BF16).ap()
    xnf_bounce = nc.dram_tensor("xnf_bounce", [D, TMY], BF16).ap()
    xnf_all = nc.dram_tensor("xnf_all", [NCORES * D, TMY], BF16).ap()
    q_scr = nc.dram_tensor("q_scr", [512, S], BF16).ap()
    g_scr = nc.dram_tensor("g_scr", [D, TMY], F32).ap()
    attn_bounce = nc.dram_tensor("attn_bounce", [512, S], BF16).ap()
    attn_all = nc.dram_tensor("attn_all", [NCORES * 512, S], BF16).ap()
    sg_scr = nc.dram_tensor("sg_scr", [D, TMY], BF16).ap()

    ctx = contextlib.ExitStack()
    with ctx:
        sb = lambda name, shape, dt: ctx.enter_context(nc.sbuf_tensor(name, shape, dt))
        AB = sb("arena_bf", [128, 59392], BF16)
        AFp = sb("arena_f32", [128, 12288], F32)

        XNF = AB[:, 0:32768]
        FFNW = [AB[:, 32768 + i * 4096:32768 + (i + 1) * 4096] for i in (0, 1)]
        AEFT = AB[:, 40960:43008]
        BEFT = AB[:, 43008:47104]
        BEFT2 = AB[:, 49152:53248]
        AXB = AB[:, 47104:48128]
        AXB2 = AB[:, 48128:49152]
        WDEQ = AB[:, 0:24576]
        SLAB = AB[:, 24576:40960]
        KSB = AB[:, 40960:49152]
        VT = AB[:, 49152:57344]
        VSB = AB[:, 57344:57856]
        QEV = [AB[:, 57856 + i * 512:57856 + (i + 1) * 512] for i in (0, 1)]
        ATTR = AB[:, 0:32768]
        QLD = [AB[:, 32768 + i * 2048:32768 + (i + 1) * 2048] for i in (0, 1)]
        PRB = [AB[:, 36864 + i * 512:36864 + (i + 1) * 512] for i in (0, 1)]
        ATTM = AB[:, 24576:57344]
        OCD = [AB[:, i * 4096:(i + 1) * 4096] for i in (0, 1)]
        WOD = [AB[:, 8192 + i * 4096:8192 + (i + 1) * 4096] for i in (0, 1)]
        SGB = [AB[:, 16384 + i * 512:16384 + (i + 1) * 512] for i in (0, 1)]
        SGT = [AB[:, i * 4096:(i + 1) * 4096] for i in (0, 1)]
        XT = [AFp[:, i * 4096:(i + 1) * 4096] for i in (0, 1)]
        SQS = AFp[:, 8192:12288]
        LAC = [AFp[:, i * 4096:(i + 1) * 4096] for i in (0, 1)]
        LLD = AFp[:, 8192:12288]
        BAC = [AFp[:, i * 1024:(i + 1) * 1024] for i in (0, 1)]
        BLD = AFp[:, 8192:9216]
        GEV = [AFp[:, 4096 + i * 512:4096 + (i + 1) * 512] for i in (0, 1)]
        RCS = [AFp[:, i * 512:(i + 1) * 512] for i in range(4)]
        RT1 = AFp[:, 2048:2560]
        RT2 = AFp[:, 2560:3072]
        RBS = AFp[:, 3072:3584]
        GLD = [AFp[:, i * 512:(i + 1) * 512] for i in (0, 1)]
        XTF = AFp[:, 0:4096]
        OUTF = AFp[:, 4096:8192]

        ident_f = sb("ident_f", [128, 128], F32)
        ident_b = sb("ident_b", [128, 128], BF16)
        masks = sb("masks", [128, 2048], BF16)
        biasT = sb("biasT", [128, 32], F32)
        gate_sb = sb("gate_sb", [1, 16], F32)
        gate_cols = sb("gate_cols", [128, 16], F32)
        scl_bf = sb("scl_bf", [1, 4096], BF16)
        halfones = sb("halfones", [1, 128], BF16)
        ones_b = sb("ones_b", [1, 128], BF16)
        ones_f = sb("ones_f", [1, 128], F32)
        ones_col = sb("ones_col", [128, 1], BF16)
        rec_f = sb("rec_f", [1, 512], F32)
        rec_b = sb("rec_b", [1, 512], BF16)
        idx_sb = sb("idx_sb", [128, 32], U32)
        stat = {nm: sb("st_" + nm, [128, 1], F32)
                for nm in ("sum", "ssq", "mean", "ex2", "varep", "sd", "rstd", "nbias")}

        psf = [ctx.enter_context(nc.psum_tensor(f"psf{i}", [128, 512], F32))
               for i in range(7)]
        psb = ctx.enter_context(nc.psum_tensor("psb", [128, 1024], BF16))

        sems = {nm: ctx.enter_context(nc.semaphore("sem_" + nm))
                for nm in ("sync", "gp", "cc", "t", "s", "v")}

        ops = {e: [] for e in ENGINES}
        cnt = {}

        def _inc(name, n=1):
            cnt[name] = cnt.get(name, 0) + n
            return cnt[name]

        def _val(name):
            return cnt.get(name, 0)

        def em(eng, fn):
            ops[eng].append(fn)

        def wait(eng, sem_name, val):
            if val and val > 0:
                em(eng, lambda e, sm=sems[sem_name], v=val: e.wait_ge(sm, v))

        def dma(eng, out, in_, **kw):
            sem_name = "sync" if eng == "sync" else "gp"
            v = _inc(sem_name, 16)
            em(eng, lambda e, o=out, i=in_, sm=sems[sem_name], kw2=dict(kw):
               e.dma_start(out=o, in_=i, **kw2).then_inc(sm, 16))
            return v

        def op1(eng, sem_name, build):
            v = _inc(sem_name)
            em(eng, lambda e, sm=sems[sem_name], b=build: b(e).then_inc(sm, 1))
            return v

        def op0(eng, build):
            em(eng, lambda e, b=build: b(e))

        def barrier():
            vals = {k: _val(k) for k in ("sync", "gp", "cc", "t", "s", "v")}
            for eng in ENGINES:
                for k, v in vals.items():
                    wait(eng, k, v)

        lm_identf = dma("sync", ident_f[:], ident_f_in)
        lm_identb = dma("sync", ident_b[:], ident_b_in)
        dma("sync", masks[:], masks_in)
        lm_bias = dma("sync", biasT[:], bias_t)
        dma("sync", idx_sb[:], gidx)
        lm_gate = dma("gpsimd", gate_sb[:], gate_in)
        op0("vector", lambda e: e.memset(halfones[:], 0.5))
        op0("vector", lambda e: e.memset(ones_b[:], 1.0))
        op0("vector", lambda e: e.memset(ones_f[:], 1.0))
        consts_v = op1("vector", "v", lambda e: e.memset(ones_col[:], 1.0))

        xst = []
        for i in range(8):
            pp = i % 2
            lv = dma("sync", XT[pp][:], x_my[i * 128:(i + 1) * 128, :])
            wait("vector", "sync", lv)
            vsum = op1("vector", "v", lambda e, p=pp: e.reduce_sum(
                stat["sum"][:], XT[p][:], axis=mybir.AxisListType.X))
            zq = op1("vector", "v", lambda e: e.memset(stat["ssq"][:], 0.0))
            wait("scalar", "sync", lv)
            wait("scalar", "v", zq)
            ssq = op1("scalar", "s", lambda e, p=pp: e.activation(
                SQS[:], XT[p][:], AF.Square, accum_out=stat["ssq"][:]))
            wait("vector", "s", ssq)
            wait("vector", "v", vsum)
            vme = op1("vector", "v", lambda e: e.tensor_scalar_mul(
                stat["mean"][:], stat["sum"][:], 1.0 / D))
            vex = op1("vector", "v", lambda e: e.tensor_scalar_mul(
                stat["ex2"][:], stat["ssq"][:], 1.0 / D))
            wait("vector", "v", vme)
            vv1 = op1("vector", "v", lambda e: e.scalar_tensor_tensor(
                stat["varep"][:], stat["mean"][:], -1.0, stat["mean"][:], OP.mult, OP.mult))
            wait("vector", "v", max(vv1, vex))
            vvar = op1("vector", "v", lambda e: e.scalar_tensor_tensor(
                stat["varep"][:], stat["ex2"][:], EPS, stat["varep"][:], OP.add, OP.add))
            wait("scalar", "v", vvar)
            vsd = op1("scalar", "s", lambda e: e.activation(
                stat["sd"][:], stat["varep"][:], AF.Sqrt))
            wait("vector", "s", vsd)
            vrs = op1("vector", "v", lambda e: e.reciprocal(stat["rstd"][:], stat["sd"][:]))
            wait("vector", "v", vrs)
            vnb = op1("vector", "v", lambda e: e.scalar_tensor_tensor(
                stat["nbias"][:], stat["mean"][:], -1.0, stat["rstd"][:], OP.mult, OP.mult))
            wait("scalar", "v", vnb)
            if i > 0:
                wait("scalar", "sync", xst[-1])
            vxn = op1("scalar", "s", lambda e, p=pp: e.activation(
                FFNW[0][:, 0:4096], XT[p][:], AF.Identity,
                bias=stat["nbias"][:], scale=stat["rstd"][:]))
            wait("sync", "s", vxn)
            xst.append(dma("sync", xn_t_scr[i * 128:(i + 1) * 128, :], FFNW[0][:, 0:4096]))
        wait("sync", "sync", xst[-1])
        xnf_ready = 0
        for f in range(NKC):
            v = _inc("sync", 16)
            em("sync", lambda e, f2=f: e.dma_start_transpose(
                XNF[:, f2 * 1024:(f2 + 1) * 1024],
                xn_t_scr[0:1024, f2 * 128:(f2 + 1) * 128]).then_inc(sems["sync"], 16))
            xnf_ready = v
        wait("gpsimd", "sync", xnf_ready)
        bl = dma("gpsimd", xnf_bounce.rearrange("(f p) t -> p f t", p=128),
                 XNF.rearrange("p (f t) -> p f t", f=NKC))
        wait("gpsimd", "gp", bl)
        cc_xn = op1("gpsimd", "cc", lambda e: e.collective_compute(
            "AllGather", OP.bypass, replica_groups=[list(range(NCORES))],
            ins=[xnf_bounce.bitcast(F32).opt()],
            outs=[xnf_all.bitcast(F32).opt()]))

        wait("tensor", "gp", lm_gate)
        wait("tensor", "v", consts_v)
        tg = op1("tensor", "t", lambda e: e.matmul(
            psf[2][:, 0:16], ones_f[:], gate_sb[:], start=True, stop=True))
        wait("vector", "t", tg)
        vg = op1("vector", "v", lambda e: e.tensor_copy(gate_cols[:], psf[2][:, 0:16]))
        gate_bcast_done = vg

        prev_stt = 0
        for n in range(16):
            if prev_stt:
                wait("gpsimd", "v", prev_stt)
            lv = dma("gpsimd", LLD[0:32, :], la_in[n])
            wait("vector", "gp", lv)
            if n == 0:
                wait("vector", "v", vg)
                prev_stt = op1("vector", "v", lambda e: e.tensor_scalar_mul(
                    LAC[0][0:32, :], LLD[0:32, :], gate_cols[0:32, 0:1]))
            else:
                src, dst = LAC[(n + 1) % 2], LAC[n % 2]
                wait("vector", "v", prev_stt)
                prev_stt = op1("vector", "v", lambda e, s2=src, d2=dst, n2=n:
                               e.scalar_tensor_tensor(
                                   d2[0:32, :], LLD[0:32, :],
                                   gate_cols[0:32, n2:n2 + 1], s2[0:32, :],
                                   OP.mult, OP.add))
        a_own = LAC[1]
        aeft_done = 0
        wait("tensor", "sync", lm_identf)
        for kc in range(NKC):
            wait("tensor", "v", prev_stt if kc == 0 else aeft_done)
            tv = op1("tensor", "t", lambda e, k2=kc: e.transpose(
                psf[5][:, 0:32], a_own[0:32, k2 * 128:(k2 + 1) * 128],
                ident_f[0:32, 0:32]))
            wait("vector", "t", tv)
            aeft_done = op1("vector", "v", lambda e, k2=kc: e.tensor_copy(
                AEFT[:, k2 * 64:k2 * 64 + 32], psf[5][:, 0:32]))
        wait("gpsimd", "v", prev_stt)
        lv = dma("gpsimd", LLD[0:32, :], las_in)
        for kc in range(NKC):
            wait("tensor", "gp", lv)
            wait("tensor", "v", aeft_done)
            tv = op1("tensor", "t", lambda e, k2=kc: e.transpose(
                psf[5][:, 0:32], LLD[0:32, k2 * 128:(k2 + 1) * 128],
                ident_f[0:32, 0:32]))
            wait("vector", "t", tv)
            aeft_done = op1("vector", "v", lambda e, k2=kc: e.tensor_copy(
                AEFT[:, k2 * 64 + 32:k2 * 64 + 64], psf[5][:, 0:32]))

        prev_b = 0
        for n in range(16):
            if n == 0:
                wait("gpsimd", "v", aeft_done)
            else:
                wait("gpsimd", "v", prev_b)
            lv = dma("gpsimd", BLD[:, :], lb_in[n].rearrange("p a b -> p (a b)"))
            wait("vector", "gp", lv)
            if n == 0:
                prev_b = op1("vector", "v", lambda e: e.tensor_scalar_mul(
                    BAC[0][:], BLD[:], gate_cols[:, 0:1]))
            else:
                src, dst = BAC[(n + 1) % 2], BAC[n % 2]
                wait("vector", "v", prev_b)
                prev_b = op1("vector", "v", lambda e, s2=src, d2=dst, n2=n:
                             e.scalar_tensor_tensor(
                                 d2[:], BLD[:], gate_cols[:, n2:n2 + 1], s2[:],
                                 OP.mult, OP.add))
        b_own = BAC[1]
        wait("gpsimd", "v", prev_b)
        bls = dma("gpsimd", BLD[:, :], lbs_in.rearrange("p a b -> p (a b)"))
        beft_done = 0
        for oc in range(NOC):
            wait("tensor", "v", prev_b if oc == 0 else beft_done)
            tv1 = op1("tensor", "t", lambda e, o2=oc: e.transpose(
                psf[5][0:32, 0:128], b_own[:, o2 * 32:(o2 + 1) * 32], ident_f[:]))
            wait("vector", "t", tv1)
            ev1 = op1("vector", "v", lambda e, o2=oc: e.tensor_copy(
                BEFT[0:32, o2 * 128:(o2 + 1) * 128], psf[5][0:32, 0:128]))
            if oc == 0:
                wait("tensor", "gp", bls)
            wait("tensor", "v", ev1)
            tv2 = op1("tensor", "t", lambda e, o2=oc: e.transpose(
                psf[5][0:32, 0:128], BLD[:, o2 * 32:(o2 + 1) * 32], ident_f[:]))
            wait("vector", "t", tv2)
            beft_done = op1("vector", "v", lambda e, o2=oc: e.tensor_copy(
                BEFT2[0:32, o2 * 128:(o2 + 1) * 128], psf[5][0:32, 0:128]))

        ax_done = 0
        wait("tensor", "sync", xnf_ready)
        for st in range(2):
            for part, dstbuf, coff in ((0, AXB, 0), (1, AXB2, 32)):
                wait("tensor", "v", aeft_done if (st == 0 and part == 0) else ax_done)
                for kc in range(NKC):
                    last = kc == NKC - 1
                    b = (lambda e, k2=kc, s2=st, c3=coff: e.matmul(
                        psf[3][0:32, :], AEFT[:, k2 * 64 + c3:k2 * 64 + c3 + 32],
                        XNF[:, k2 * 1024 + s2 * 512:k2 * 1024 + s2 * 512 + 512],
                        start=(k2 == 0), stop=(k2 == NKC - 1), skip_group_check=True))
                    if last:
                        tv = op1("tensor", "t", b)
                    else:
                        op0("tensor", b)
                wait("vector", "t", tv)
                ax_done = op1("vector", "v", lambda e, s2=st, d2=dstbuf: e.tensor_copy(
                    d2[0:32, s2 * 512:(s2 + 1) * 512], psf[3][0:32, :]))
        wait("tensor", "v", ax_done)

        wait("scalar", "sync", lm_bias)
        wait("gpsimd", "sync", xst[-1])
        g_marks = []
        gst_marks = []
        wslab_mark = [0, 0]
        for oc in range(NOC):
            pp = oc % 2
            if wslab_mark[pp]:
                wait("gpsimd", "t", wslab_mark[pp])
            lv = dma("gpsimd", FFNW[pp][:].rearrange("p (a b) -> p a b", a=NKC), wker[oc])
            wait("tensor", "gp", lv)
            wait("tensor", "v", max(ax_done, beft_done))
            for st in range(2):
                bank = (oc * 2 + st) % 2
                if len(g_marks) >= 2:
                    wait("tensor", "s", g_marks[-2])
                for kc in range(NKC):
                    op0("tensor", lambda e, k2=kc, p2=pp, s2=st, b2=bank: e.matmul(
                        psf[b2][:, :], FFNW[p2][:, k2 * 128:(k2 + 1) * 128],
                        XNF[:, k2 * 1024 + s2 * 512:k2 * 1024 + s2 * 512 + 512],
                        start=(k2 == 0), stop=False, skip_group_check=True))
                op0("tensor", lambda e, o2=oc, s2=st, b2=bank: e.matmul(
                    psf[b2][:, :], BEFT[0:32, o2 * 128:(o2 + 1) * 128],
                    AXB[0:32, s2 * 512:(s2 + 1) * 512],
                    start=False, stop=False, skip_group_check=True))
                tmm = op1("tensor", "t", lambda e, o2=oc, s2=st, b2=bank: e.matmul(
                    psf[b2][:, :], BEFT2[0:32, o2 * 128:(o2 + 1) * 128],
                    AXB2[0:32, s2 * 512:(s2 + 1) * 512],
                    start=False, stop=True, skip_group_check=True))
                wslab_mark[pp] = tmm
                gp2 = (oc * 2 + st) % 2
                if len(gst_marks) >= 2:
                    wait("scalar", "sync", gst_marks[-2])
                wait("scalar", "t", tmm)
                gm = op1("scalar", "s", lambda e, o2=oc, b2=bank, g3=gp2: e.activation(
                    GEV[g3][:], psf[b2][:, :], AF.Silu,
                    bias=biasT[:, o2:o2 + 1], scale=1.0))
                g_marks.append(gm)
                wait("sync", "s", gm)
                gst_marks.append(dma(
                    "sync", g_scr[oc * 128:(oc + 1) * 128, st * 512:(st + 1) * 512],
                    GEV[gp2][:]))
        barrier()

        last_bcast_t = [0]

        def emit_dequant(dst, codes_ap, scales_ap, ocd_buf, stage_war=0):
            src_buf = ocd_buf
            if stage_war:
                wait("gpsimd", "v", stage_war)
            lv = dma("gpsimd", src_buf.rearrange("p (a b) -> p a b", a=NKC), codes_ap)
            if last_bcast_t[0]:
                wait("gpsimd", "t", last_bcast_t[0])
            sv = dma("gpsimd", scl_bf[:].rearrange("o (a b) -> o a b", a=NKC), scales_ap)
            vmark = 0
            for piece in range(8):
                wait("tensor", "gp", sv)
                if vmark:
                    wait("tensor", "v", vmark)
                tv = op1("tensor", "t", lambda e, p2=piece: e.matmul(
                    psf[2][:, :], halfones[:], scl_bf[:, p2 * 512:(p2 + 1) * 512],
                    start=True, stop=True, skip_group_check=True))
                last_bcast_t[0] = tv
                wait("vector", "t", tv)
                wait("vector", "gp", lv)
                vmark = op1("vector", "v", lambda e, p2=piece, d2=dst, s3=src_buf:
                            e.scalar_tensor_tensor(
                                d2[:, p2 * 512:(p2 + 1) * 512],
                                s3[:, p2 * 512:(p2 + 1) * 512],
                                -8.0, psf[2][:, :], OP.add, OP.mult))
            return vmark

        stg = SLAB[:, 0:4096]
        dqw = 0
        for mc in range(HM):
            dqw = emit_dequant(WDEQ[:, mc * 4096:(mc + 1) * 4096], qc[mc],
                               qs_[mc].unsqueeze(0), stg, stage_war=dqw)
        dqw = emit_dequant(WDEQ[:, 16384:20480], kc_[0], ks_[0].unsqueeze(0),
                           stg, stage_war=dqw)
        deq_done = emit_dequant(WDEQ[:, 20480:24576], vc_[0], vs_[0].unsqueeze(0),
                                stg, stage_war=dqw)
        slab_consumed0 = deq_done

        wait("sync", "cc", cc_xn)
        xa4 = xnf_all.rearrange("(j kc p) t -> j p kc t", p=128, kc=NKC)
        slab_consumed = 0
        rope_use = [0, 0]
        qev_store = [0, 0]
        vt_last = 0
        k_last = 0
        q_stores_done = 0
        pj_bank_war = [0, 0]
        SLABS = [SLAB, AFp[:, 3584:11776].bitcast(BF16)]
        slab_war = [0, 0]
        for ts in range(NTS):
            j, tl = ts // 2, (ts % 2) * 512
            spp = ts % 2
            cur_slab = SLABS[spp]
            wait("sync", "t", slab_war[spp])
            if ts == 0:
                wait("sync", "v", slab_consumed0)
            slv = dma("sync", cur_slab[:].rearrange("p (a b) -> p a b", a=NKC),
                      xa4[j][:, :, tl:tl + 512])
            rpp = ts % 2
            if rope_use[rpp]:
                wait("sync", "v", rope_use[rpp])
            cv = dma("sync", RCS[rpp * 2][:], rope_c[:, ts * 512:(ts + 1) * 512])
            sv2 = dma("sync", RCS[rpp * 2 + 1][:], rope_s[:, ts * 512:(ts + 1) * 512])
            for m in range(6):
                bank = m % 2
                wcol = m * 4096 if m < 4 else (16384 if m == 4 else 20480)
                wait("tensor", "sync", slv)
                if ts == 0 and m == 0:
                    wait("tensor", "v", deq_done)
                wait("tensor", "v", pj_bank_war[bank])
                for kc in range(NKC):
                    last = kc == NKC - 1
                    b = (lambda e, k2=kc, w2=wcol, b2=bank, sl=cur_slab: e.matmul(
                        psf[b2][:, :], WDEQ[:, w2 + k2 * 128:w2 + (k2 + 1) * 128],
                        sl[:, k2 * 512:(k2 + 1) * 512],
                        start=(k2 == 0), stop=(k2 == NKC - 1), skip_group_check=True))
                    if last:
                        tmm = op1("tensor", "t", b)
                    else:
                        op0("tensor", b)
                if m == 5:
                    slab_war[spp] = tmm
                if m < 5:
                    wait("vector", "t", tmm)
                    wait("vector", "sync", sv2)
                    op1("vector", "v", lambda e, b2=bank, r2=rpp: e.tensor_tensor(
                        RT1[:], psf[b2][:, :], RCS[r2 * 2][:], OP.mult))
                    op1("vector", "v", lambda e, b2=bank, r2=rpp: e.tensor_tensor(
                        RT2[0:64, :], psf[b2][64:128, :], RCS[r2 * 2 + 1][0:64, :], OP.mult))
                    vrt = op1("vector", "v", lambda e, b2=bank, r2=rpp: e.tensor_tensor(
                        RT2[64:128, :], psf[b2][0:64, :], RCS[r2 * 2 + 1][64:128, :], OP.mult))
                    wait("vector", "v", vrt)
                    if m < 4:
                        qp = m % 2
                        if qev_store[qp]:
                            wait("vector", "sync", qev_store[qp])
                        vev = op1("vector", "v", lambda e, q2=qp: e.tensor_add(
                            QEV[q2][:], RT1[:], RT2[:]))
                        wait("sync", "v", vev)
                        qev_store[qp] = dma(
                            "sync", q_scr[m * 128:(m + 1) * 128, ts * 512:(ts + 1) * 512],
                            QEV[qp][:])
                        q_stores_done = qev_store[qp]
                    else:
                        vev = op1("vector", "v", lambda e, t2=ts: e.tensor_add(
                            KSB[:, t2 * 512:(t2 + 1) * 512], RT1[:], RT2[:]))
                        k_last = vev
                    pj_bank_war[bank] = vev
                    rope_use[rpp] = vev
                else:
                    wait("vector", "t", tmm)
                    vev = op1("vector", "v", lambda e, b2=bank: e.tensor_copy(
                        VSB[:], psf[b2][:, :]))
                    pj_bank_war[bank] = vev
                    wait("tensor", "v", vev)
                    vtr = 0
                    for tb in range(4):
                        if vtr:
                            wait("tensor", "v", vtr)
                        ttr = op1("tensor", "t", lambda e, t3=tb: e.transpose(
                            psb[:, 0:128], VSB[:, t3 * 128:(t3 + 1) * 128], ident_b[:]))
                        wait("vector", "t", ttr)
                        vtr = op1("vector", "v", lambda e, t2=ts, t3=tb: e.tensor_copy(
                            VT[:, (t2 * 4 + t3) * 128:(t2 * 4 + t3 + 1) * 128],
                            psb[:, 0:128]))
                    vt_last = vtr
        kv_ready = max(k_last, vt_last)
        barrier()

        scdone = {}
        expdone = {}
        maskdone = {}
        jg = 0
        last_fin_v = 0
        last_tbc_s = 0
        last_tbc_t = 0
        qld_war = [0, 0]
        for qs in range(NTS):
            qpp = qs % 2
            q4 = q_scr.rearrange("(h p) t -> p h t", p=128)
            if qld_war[qpp]:
                wait("sync", "t", qld_war[qpp])
            wait("sync", "sync", q_stores_done)
            qlv = dma("sync", QLD[qpp][:].rearrange("p (a b) -> p a b", a=HM),
                      q4[:, :, qs * 512:(qs + 1) * 512])
            for hp in range(2):
                heads = (hp * 2, hp * 2 + 1)
                tiles = [(ks, hh, tb) for ks in range(qs + 1)
                         for hh in range(2) for tb in range(4)]
                n = len(tiles)
                first_av = [True, True]
                sweep_js = []

                def emit_scores(jj, tile):
                    ks, hh, tb = tile
                    h = heads[hh]
                    if jj - 2 in expdone:
                        wait("tensor", "s", expdone[jj - 2])
                    if jj == sweep_js[0]:
                        wait("tensor", "v", max(kv_ready, last_fin_v))
                        wait("tensor", "sync", qlv)
                    kb = ks * 4 + tb
                    scdone[jj] = op1("tensor", "t", lambda e, b2=jj % 2, k2=kb, h2=h,
                                     q2=qpp: e.matmul(
                        psf[b2][:, :], KSB[:, k2 * 128:(k2 + 1) * 128],
                        QLD[q2][:, h2 * 512:(h2 + 1) * 512],
                        start=True, stop=True, skip_group_check=True))

                def emit_av(jj, tile):
                    ks, hh, tb = tile
                    if ks == qs:
                        wait("tensor", "gp", maskdone[jj])
                    else:
                        wait("tensor", "s", expdone[jj])
                    kb = ks * 4 + tb
                    sf = first_av[hh]
                    first_av[hh] = False
                    op0("tensor", lambda e, k2=kb, h3=hh, p2=jj % 2, s2=sf: e.matmul(
                        psf[2 + h3][:, :], VT[:, k2 * 128:(k2 + 1) * 128],
                        PRB[p2][:], start=s2, stop=False, skip_group_check=True))
                    return op1("tensor", "t", lambda e, h3=hh, p2=jj % 2, s2=sf: e.matmul(
                        psf[4 + h3][0:1, :], ones_col[:], PRB[p2][:],
                        start=s2, stop=False, skip_group_check=True))

                for li, tile in enumerate(tiles):
                    jj = jg + li
                    sweep_js.append(jj)
                    emit_scores(jj, tile)
                    ks, hh, tb = tile
                    wait("scalar", "t", scdone[jj])
                    expdone[jj] = op1("scalar", "s", lambda e, p2=jj % 2, b2=jj % 2:
                                      e.activation(PRB[p2][:], psf[b2][:, :],
                                                   AF.Exp, scale=ISQ))
                    if ks == qs:
                        wait("gpsimd", "s", expdone[jj])
                        maskdone[jj] = op1("gpsimd", "gp", lambda e, p2=jj % 2, t3=tb:
                                           e.tensor_tensor(
                                               PRB[p2][:], PRB[p2][:],
                                               masks[:, t3 * 512:(t3 + 1) * 512],
                                               OP.mult))
                    if li >= 1:
                        emit_av(jg + li - 1, tiles[li - 1])
                av_last = emit_av(jg + n - 1, tiles[n - 1])
                qld_war[qpp] = av_last
                jg += n
                for hh in range(2):
                    h = heads[hh]
                    wait("vector", "t", av_last)
                    if last_tbc_t:
                        wait("vector", "t", last_tbc_t)
                    vrc = op1("vector", "v", lambda e, h3=hh: e.reciprocal(
                        rec_f[:], psf[4 + h3][0:1, :]))
                    wait("vector", "v", vrc)
                    vrec = op1("vector", "v", lambda e: e.tensor_copy(rec_b[:], rec_f[:]))
                    wait("tensor", "v", vrec)
                    if last_tbc_s:
                        wait("tensor", "s", last_tbc_s)
                    tbc = op1("tensor", "t", lambda e: e.matmul(
                        psf[6][:, :], ones_b[:], rec_b[:], start=True, stop=True,
                        skip_group_check=True))
                    last_tbc_t = tbc
                    wait("scalar", "t", tbc)
                    if last_fin_v:
                        wait("scalar", "v", last_fin_v)
                    last_tbc_s = op1("scalar", "s", lambda e: e.copy(RBS[:], psf[6][:, :]))
                    wait("vector", "s", last_tbc_s)
                    last_fin_v = op1("vector", "v", lambda e, h3=hh, h2=h, q2=qs:
                                     e.tensor_tensor(
                                         ATTR[:, (h2 * NTS + q2) * 512:
                                              (h2 * NTS + q2 + 1) * 512],
                                         psf[2 + h3][:, :], RBS[:], OP.mult))
        wait("gpsimd", "v", last_fin_v)
        abl = dma("gpsimd", attn_bounce.rearrange("(h p) t -> p h t", p=128),
                  ATTR.rearrange("p (h t) -> p h t", h=HM))
        wait("gpsimd", "gp", abl)
        cc_at = op1("gpsimd", "cc", lambda e: e.collective_compute(
            "AllGather", OP.bypass, replica_groups=[list(range(NCORES))],
            ins=[attn_bounce.bitcast(F32).opt()],
            outs=[attn_all.bitcast(F32).opt()]))
        barrier()

        wait("gpsimd", "cc", cc_at)
        att_virt = attn_all.rearrange("r (v c) -> (r v) c", c=1024)
        gather_done = 0
        for m in range(NKC):
            v = _inc("gp", 16)
            em("gpsimd", lambda e, m2=m: e.indirect_dma_start(
                ATTM[:, m2 * 1024:(m2 + 1) * 1024], None, att_virt,
                IndirectOffsetOnAxis(idx_sb[:, m2:m2 + 1], 0)).then_inc(sems["gp"], 16))
            gather_done = v

        sg_marks = []
        sgst_marks = []
        gld_war = [0, 0]
        ocd_war = [0, 0]
        wod_war = [0, 0]
        last_bcast_t[0] = 0
        for oc in range(NOC):
            pp = oc % 2
            if ocd_war[pp]:
                wait("gpsimd", "v", ocd_war[pp])
            if wod_war[pp]:
                wait("vector", "t", wod_war[pp])
            deq_mark = emit_dequant(WOD[pp], oc_[oc], os_[oc].unsqueeze(0),
                                    ocd_buf=OCD[pp])
            ocd_war[pp] = deq_mark
            for st in range(2):
                bank = st
                wait("tensor", "v", deq_mark)
                if oc == 0 and st == 0:
                    wait("tensor", "gp", gather_done)
                if len(sg_marks) >= 2:
                    wait("tensor", "v", sg_marks[-2])
                for kc in range(NKC):
                    last = kc == NKC - 1
                    b = (lambda e, k2=kc, c2=pp, s2=st, b2=bank: e.matmul(
                        psf[b2][:, :], WOD[c2][:, k2 * 128:(k2 + 1) * 128],
                        ATTM[:, k2 * 1024 + s2 * 512:k2 * 1024 + s2 * 512 + 512],
                        start=(k2 == 0), stop=(k2 == NKC - 1), skip_group_check=True))
                    if last:
                        tmm = op1("tensor", "t", b)
                    else:
                        op0("tensor", b)
                wod_war[pp] = tmm
                if gld_war[st]:
                    wait("sync", "v", gld_war[st])
                glv = dma("sync", GLD[st][:],
                          g_scr[oc * 128:(oc + 1) * 128, st * 512:(st + 1) * 512])
                sgp = (oc * 2 + st) % 2
                wait("vector", "t", tmm)
                wait("vector", "sync", glv)
                if len(sgst_marks) >= 2:
                    wait("vector", "sync", sgst_marks[-2])
                sgm = op1("vector", "v", lambda e, b2=bank, s2=st, c2=sgp: e.tensor_tensor(
                    SGB[c2][:], psf[b2][:, :], GLD[s2][:], OP.mult))
                sg_marks.append(sgm)
                gld_war[st] = sgm
                wait("sync", "v", sgm)
                sgst_marks.append(dma(
                    "sync", sg_scr[oc * 128:(oc + 1) * 128, st * 512:(st + 1) * 512],
                    SGB[sgp][:]))
        sg_stores_done = sgst_marks[-1]
        barrier()

        out_stored = 0
        sgt_war = [0, 0]
        prev_add = 0
        for tc in range(8):
            pp = tc % 2
            if tc == 0:
                wait("sync", "sync", sg_stores_done)
            if sgt_war[pp]:
                wait("sync", "v", sgt_war[pp])
            v = _inc("sync", 16)
            em("sync", lambda e, t2=tc, c2=pp: e.dma_start_transpose(
                SGT[c2][:], sg_scr[0:D, t2 * 128:(t2 + 1) * 128]).then_inc(sems["sync"], 16))
            sgtl = v
            if prev_add:
                wait("sync", "v", prev_add)
            xlv = dma("sync", XTF[:], x_my[tc * 128:(tc + 1) * 128, :])
            wait("vector", "sync", xlv)
            wait("vector", "sync", sgtl)
            if out_stored:
                wait("vector", "sync", out_stored)
            prev_add = op1("vector", "v", lambda e, c2=pp: e.tensor_add(
                OUTF[:], XTF[:], SGT[c2][:]))
            sgt_war[pp] = prev_add
            wait("sync", "v", prev_add)
            out_stored = dma("sync", out_my[tc * 128:(tc + 1) * 128, :], OUTF[:])

        if debug:
            wait("gpsimd", "sync", out_stored)
            dma("gpsimd", dumps["d_xn"], xn_t_scr)
            dma("gpsimd", dumps["d_q"], q_scr)
            dma("gpsimd", dumps["d_g"], g_scr)
            dma("gpsimd", dumps["d_attn"], attn_bounce)
            dv = dma("gpsimd", dumps["d_sg"], sg_scr)
            wait("gpsimd", "gp", dv)
        wait("sync", "sync", out_stored)

        block = ctx.enter_context(nc.Block())

        @block.sync
        def _(e):
            for f in ops["sync"]:
                f(e)

        @block.gpsimd
        def _(e):
            for f in ops["gpsimd"]:
                f(e)

        @block.vector
        def _(e):
            for f in ops["vector"]:
                f(e)

        @block.scalar
        def _(e):
            for f in ops["scalar"]:
                f(e)

        @block.tensor
        def _(e):
            for f in ops["tensor"]:
                f(e)

    return nc


def _host_prep(inputs):
    f32 = np.float32
    x = np.ascontiguousarray(np.asarray(inputs["L0_attn_x"], dtype=f32).reshape(S, D))
    qcodes = np.asarray(inputs["L0_attn_q_codes"], dtype=np.int32)
    kcodes = np.asarray(inputs["L0_attn_k_codes"], dtype=np.int32)
    vcodes = np.asarray(inputs["L0_attn_v_codes"], dtype=np.int32)
    ocodes = np.asarray(inputs["L0_attn_o_codes"], dtype=np.int32)
    qsc = np.asarray(inputs["L0_attn_q_scales"], dtype=f32)
    ksc = np.asarray(inputs["L0_attn_k_scales"], dtype=f32)
    vsc = np.asarray(inputs["L0_attn_v_scales"], dtype=f32)
    osc = np.asarray(inputs["L0_attn_o_scales"], dtype=f32)
    gate = np.asarray(inputs["L0_ffn_gate"], dtype=f32)
    kern = np.asarray(inputs["L0_ffn_kernel"], dtype=f32)
    bias = np.asarray(inputs["L0_ffn_bias"], dtype=f32)
    lA = np.ascontiguousarray(np.asarray(inputs["L0_ffn_lora_A"], dtype=f32))
    lB = np.asarray(inputs["L0_ffn_lora_B"], dtype=f32)
    lAs = np.ascontiguousarray(np.asarray(inputs["L0_ffn_lora_A_shared"], dtype=f32))
    lBs = np.asarray(inputs["L0_ffn_lora_B_shared"], dtype=f32)

    def colslab(codes, nslab):
        return np.ascontiguousarray(
            codes.reshape(NKC, 128, nslab, 128).transpose(2, 1, 0, 3))

    def sc_slab(scales, nslab):
        return np.ascontiguousarray(
            scales.reshape(NKC, nslab, 128).transpose(1, 0, 2))

    kern_h = colslab(kern, NOC)
    oc_h = colslab(ocodes, NOC)
    os_h = sc_slab(osc, NOC)
    bias_T = np.ascontiguousarray(bias.reshape(32, 128).T)
    lb_h = np.ascontiguousarray(lB.reshape(16, 32, 128, 32).transpose(0, 2, 1, 3))
    lbs_h = np.ascontiguousarray(lBs.reshape(32, 128, 32).transpose(1, 0, 2))

    half = DH // 2
    inv = 1.0 / (10000.0 ** (np.arange(half, dtype=np.float64) / half))
    ang = np.arange(S, dtype=np.float64)[:, None] * inv[None, :]
    cosT = np.cos(ang).T.astype(f32)
    sinT = np.sin(ang).T.astype(f32)
    rope_cos = np.ascontiguousarray(np.concatenate([cosT, cosT], axis=0))
    rope_sin = np.ascontiguousarray(np.concatenate([-sinT, sinT], axis=0))

    masks = np.zeros((128, 4, 512), dtype=f32)
    kk = np.arange(128)[:, None]
    qq = np.arange(512)[None, :]
    for tb in range(4):
        masks[:, tb, :] = (qq >= tb * 128 + kk).astype(f32)
    masks_b = np.ascontiguousarray(masks.reshape(128, 2048)).astype(ml_dtypes.bfloat16)

    ident = np.eye(128, dtype=f32)
    ident_b = ident.astype(ml_dtypes.bfloat16)

    qc_all = colslab(qcodes, 32)
    qs_all = sc_slab(qsc, 32)
    kc_all = colslab(kcodes, 8)
    ks_all = sc_slab(ksc, 8)
    vc_all = colslab(vcodes, 8)
    vs_all = sc_slab(vsc, 8)

    in_maps = []
    p = np.arange(128, dtype=np.uint32)[:, None]
    m = np.arange(32, dtype=np.uint32)[None, :]
    for c in range(NCORES):
        gidx_c = np.ascontiguousarray(((m * 128 + p) * 8 + c).astype(np.uint32))
        in_maps.append({
            "x_my": np.ascontiguousarray(x[c * TMY:(c + 1) * TMY]),
            "qc": np.ascontiguousarray(qc_all[c * HM:(c + 1) * HM]),
            "qs_": np.ascontiguousarray(qs_all[c * HM:(c + 1) * HM]),
            "kc_": np.ascontiguousarray(kc_all[c:c + 1]),
            "ks_": np.ascontiguousarray(ks_all[c:c + 1]),
            "vc_": np.ascontiguousarray(vc_all[c:c + 1]),
            "vs_": np.ascontiguousarray(vs_all[c:c + 1]),
            "oc_": oc_h,
            "os_": os_h,
            "wker": kern_h,
            "bias_t": bias_T,
            "gate_in": np.ascontiguousarray(gate[c:c + 1]),
            "la_in": lA,
            "lb_in": lb_h,
            "las_in": lAs,
            "lbs_in": lbs_h,
            "rope_c": rope_cos,
            "rope_s": rope_sin,
            "masks_in": masks_b,
            "ident_f_in": ident,
            "ident_b_in": ident_b,
            "gidx": gidx_c,
        })
    return in_maps


def _run(inputs, debug=False):
    in_maps = _host_prep(inputs)
    nc = build_nc(debug=debug)
    return run_bass_kernel_spmd(nc, in_maps, core_ids=list(range(NCORES)))


def kernel(**inputs):
    res = _run(inputs, debug=False)
    out = np.empty((1, S, D), dtype=np.float32)
    for c in range(NCORES):
        out[0, c * TMY:(c + 1) * TMY, :] = np.asarray(res.results[c]["out_my"])
    return out
